# version 03 (final)
# baseline (speedup 1.0000x reference)
import numpy as np
import ml_dtypes
import concourse.bass as bass
import concourse.mybir as mybir
from concourse.bass_utils import run_bass_kernel_spmd
from contextlib import ExitStack

F32 = mybir.dt.float32
BF16 = mybir.dt.bfloat16
U32 = mybir.dt.uint32
AF = mybir.ActivationFunctionType
ALU = mybir.AluOpType
AX = mybir.AxisListType

NC = 8
D = 1024
DI = 2048
DS = 16
TQ = 2048
TB = 8192
CH = 1024
DQ = 512
NDT = 4
NKD = 8
EPS = 1e-5

_STOP = 99
_DUMP = []


def _prep_core_inputs(inputs, c):
    g = lambda k: np.asarray(inputs[k], dtype=np.float32)
    b, dq = c // 4, c % 4
    l0 = dq * TQ
    m = {}
    x = g('L0_attn_x')
    m['xT'] = np.ascontiguousarray(x[b, l0:l0 + TQ, :].T)
    m['qoff'] = np.array([[dq * TQ]], dtype=np.uint32)
    for i in (1, 2):
        p = f'L0_attn_m{i}_'
        sl = slice(dq * DQ, (dq + 1) * DQ)
        in_w = g(p + 'in_w')
        m[f'm{i}wxsT'] = np.ascontiguousarray(in_w[sl, :].T)
        m[f'm{i}wzT'] = np.ascontiguousarray(in_w[DI + dq * DQ: DI + (dq + 1) * DQ, :].T)
        m[f'm{i}cw'] = np.ascontiguousarray(
            g(p + 'conv_w')[sl].reshape(NDT, 128, 4).transpose(1, 0, 2).reshape(128, NDT * 4))
        m[f'm{i}cb'] = np.ascontiguousarray(g(p + 'conv_b')[sl].reshape(NDT, 128).T)
        m[f'm{i}alog'] = np.ascontiguousarray(
            g(p + 'Alog')[sl].reshape(NDT, 128, DS).transpose(1, 0, 2).reshape(128, NDT * DS))
        m[f'm{i}dv'] = np.ascontiguousarray(g(p + 'D')[sl].reshape(NDT, 128).T)
        m[f'm{i}dtwa'] = np.ascontiguousarray(
            np.concatenate([g(p + 'dt_w')[sl].T, g(p + 'dt_b')[sl][None, :]], axis=0))
        xp = g(p + 'xproj_w')[:, sl]
        xpa = np.zeros((DQ, 104), dtype=np.float32)
        xpa[:, 0:64] = xp[0:64].T
        xpa[:, 65:81] = xp[64:80].T
        xpa[:, 81:97] = xp[80:96].T
        m[f'm{i}xpa'] = xpa
        m[f'm{i}owT'] = np.ascontiguousarray(g(p + 'out_w').T)
    m['ng'] = np.ascontiguousarray(g('L0_attn_norm_g').reshape(NKD, 128).T)
    m['nb'] = np.ascontiguousarray(g('L0_attn_norm_b').reshape(NKD, 128).T)
    m['fg'] = np.ascontiguousarray(g('L0_attn_ffn_g').reshape(NKD, 128).T)
    m['fb'] = np.ascontiguousarray(g('L0_attn_ffn_b').reshape(NKD, 128).T)
    m['ff1T'] = np.ascontiguousarray(g('L0_attn_ff1_w').T)
    m['ff1bc'] = np.ascontiguousarray(g('L0_attn_ff1_b').reshape(32, 128).T)
    m['ff2T'] = np.ascontiguousarray(g('L0_attn_ff2_w').T)
    m['ff2bc'] = np.ascontiguousarray(g('L0_attn_ff2_b').reshape(NKD, 128).T)
    m['w1'] = g('L0_ffn_W1')
    m['w2'] = g('L0_ffn_W2')
    m['b1c'] = np.ascontiguousarray(g('L0_ffn_b1').reshape(8, 32, 128).swapaxes(1, 2))
    m['b2c'] = np.ascontiguousarray(g('L0_ffn_b2').reshape(8, NKD, 128).swapaxes(1, 2))
    m['wr'] = np.ascontiguousarray(
        g('L0_ffn_Wr').reshape(NKD, 128, 8).transpose(1, 0, 2).reshape(128, NKD * 8))
    m['br'] = np.ascontiguousarray(g('L0_ffn_br').reshape(1, 8))
    return m


class S:
    def __init__(self, stack, nc, name):
        self.h = stack.enter_context(nc.semaphore(name))
        self.n = 0
    def wait(self, eng, val=None):
        v = self.n if val is None else val
        if v > 0:
            eng.wait_ge(self.h, v)


def build(stop=None, dump=None):
    stop = _STOP if stop is None else stop
    dump = _DUMP if dump is None else dump
    nc = bass.Bass("TRN2", target_bir_lowering=False, detect_race_conditions=False)
    stk = ExitStack()

    I = {}
    def ein(name, shape, dt=F32):
        I[name] = nc.dram_tensor(name, list(shape), dt, kind="ExternalInput")
    ein('xT', (D, TQ))
    ein('qoff', (1, 1), U32)
    for i in (1, 2):
        ein(f'm{i}wxsT', (D, DQ)); ein(f'm{i}wzT', (D, DQ))
        ein(f'm{i}cw', (128, NDT * 4)); ein(f'm{i}cb', (128, NDT))
        ein(f'm{i}alog', (128, NDT * DS)); ein(f'm{i}dv', (128, NDT))
        ein(f'm{i}dtwa', (65, DQ)); ein(f'm{i}xpa', (DQ, 104))
        ein(f'm{i}owT', (DI, D))
    for k in ('ng', 'nb', 'fg', 'fb'):
        ein(k, (128, NKD))
    ein('ff1T', (D, 4096)); ein('ff1bc', (128, 32))
    ein('ff2T', (4096, D)); ein('ff2bc', (128, NKD))
    ein('w1', (8, D, 4096)); ein('w2', (8, 4096, D))
    ein('b1c', (8, 128, 32)); ein('b2c', (8, 128, NKD))
    ein('wr', (128, NKD * 8)); ein('br', (1, 8))

    out = nc.dram_tensor('out', [D, TQ], F32, kind="ExternalOutput")

    DR = {}
    def dram(name, shape, dt=F32):
        DR[name] = nc.dram_tensor(name, list(shape), dt)
        return DR[name]
    xres = dram('xres', (D, TQ))
    ubuf = dram('ubuf', (D, TQ))
    lnbuf = dram('lnbuf', (D, TQ), BF16)
    h1buf = dram('h1buf', (D, TQ), BF16)
    hall = dram('hall', (4, D, TQ), BF16)
    lnrows1 = dram('lnrows1', (2, TQ))
    dblp = [[dram(f'dblp{m}_{c}', (104, CH)) for c in range(TB // CH)] for m in range(2)]
    dblo = [[dram(f'dblo{m}_{c}', (104, CH)) for c in range(TB // CH)] for m in range(2)]
    yin = dram('yin', (2, DQ, TB), BF16)
    yout = dram('yout', (2 * 4 * DQ, TB), BF16)
    scp = dram('scp', (1, 8))
    sco = dram('sco', (1, 8))
    rowb = dram('rowb', (4, 512))
    dbgA = dram('dbgA', (128, 32))
    dbgB = dram('dbgB', (128, 128))

    mv = nc.alloc_sbuf_tensor('mv', [128, 8192], BF16)
    wA = nc.alloc_sbuf_tensor('wA', [128, 32768], BF16)
    wB = nc.alloc_sbuf_tensor('wB', [128, 32768], BF16)
    scr = nc.alloc_sbuf_tensor('scr', [128, 8192], F32)
    murep = nc.alloc_sbuf_tensor('murep', [128, 512], F32)
    rrep = nc.alloc_sbuf_tensor('rrep', [128, 512], F32)
    dblsb = nc.alloc_sbuf_tensor('dblsb', [65, CH], F32)
    stt8 = nc.alloc_sbuf_tensor('stt8', [128, 1536], F32)
    cvb = nc.alloc_sbuf_tensor('cvb', [128, 2052], F32)
    cst2 = nc.alloc_sbuf_tensor('cst2', [128, 320], F32)
    xpasb = nc.alloc_sbuf_tensor('xpasb', [128, 4 * 104], F32)
    dtwasb = nc.alloc_sbuf_tensor('dtwasb', [65, DQ], F32)
    wrsb = nc.alloc_sbuf_tensor('wrsb', [128, NKD * 8], F32)
    brsb = nc.alloc_sbuf_tensor('brsb', [1, 8], F32)
    onesc = nc.alloc_sbuf_tensor('onesc', [128, 1], F32)
    iota8 = nc.alloc_sbuf_tensor('iota8', [1, 8], F32)
    iotap = nc.alloc_sbuf_tensor('iotap', [128, 1], F32)
    sidx = nc.alloc_sbuf_tensor('sidx', [128, 4], U32)
    sidxf = nc.alloc_sbuf_tensor('sidxf', [128, 4], F32)
    scsb = nc.alloc_sbuf_tensor('scsb', [1, 16], F32)
    state = nc.alloc_sbuf_tensor('state', [128, 160], F32)
    qoffs = nc.alloc_sbuf_tensor('qoffs', [1, 1], U32)

    scrh = scr.ap().bitcast(BF16)
    wAf = wA.ap().bitcast(F32)
    wBf = wB.ap().bitcast(F32)
    rows = cvb[0:1, 0:2048]
    XRAW = scr[:, 4096:8192]
    xpst = cvb[0:104, 0:1024]

    C_CW = {1: 0, 2: 16}; C_CB = {1: 32, 2: 36}
    C_A = {1: 48, 2: 112}; C_DV = {1: 176, 2: 180}
    C_NG, C_NB, C_FG, C_FB = 184, 192, 200, 208
    C_B1, C_B2, C_F1B, C_F2B = 216, 248, 256, 288

    PB = [nc.alloc_psum_tensor(f'pb{i}', [128, 512], F32) for i in range(8)]
    PS1 = PB[6][0:1, :]
    PS2 = PB[7][0:1, :]
    PSB = PB[7][:, 0:16]

    XS = lambda j: wAf[:, j * CH:(j + 1) * CH]
    ZZ = lambda j: wAf[:, 4096 + j * CH: 4096 + (j + 1) * CH]
    DT = lambda j: wAf[:, 8192 + j * CH: 8192 + (j + 1) * CH]
    DTX = lambda j: wAf[:, 12288 + j * CH: 12288 + (j + 1) * CH]
    HM = lambda kt: mv[:, kt * CH:(kt + 1) * CH]
    DAB = [wBf[:, 4096:5120], wBf[:, 5120:6144]]
    DBX = wBf[:, 6144:7168]
    HH = wBf[:, 7168:8192]
    YA = lambda j: wBf[:, 8192 + j * CH: 8192 + (j + 1) * CH]
    REPB = [wBf[:, 12288:14336], wBf[:, 14336:16384]]

    dumps = []
    for name in dump:
        t = DR[name]
        o = nc.dram_tensor('dbg_' + name, list(t.shape), t.dtype, kind="ExternalOutput")
        dumps.append((t, o))

    GROUPS2 = [[0, 1, 2, 3], [4, 5, 6, 7]]

    with stk:
        LD = S(stk, nc, 'LD'); GD = S(stk, nc, 'GD'); CC = S(stk, nc, 'CC')
        MM = S(stk, nc, 'MM'); VE = S(stk, nc, 'VE'); SC = S(stk, nc, 'SC')
        GP = S(stk, nc, 'GP'); ST = S(stk, nc, 'ST'); REP = S(stk, nc, 'REP')
        RC = S(stk, nc, 'RC')
        block = stk.enter_context(nc.Block())

        def p0_gp(g):
            g.iota(iota8[:, :], [[1, 8]], channel_multiplier=0,
                   allow_small_or_imprecise_dtypes=True)
            g.iota(iotap[:, :], [[1, 1]], channel_multiplier=1,
                   allow_small_or_imprecise_dtypes=True)
            g.memset(onesc[:, :], 1.0)
            g.memset(state[:, :], 0.0)
            g.memset(dblsb[64:65, :], 1.0)
            g.memset(cst2[:, 318:319], EPS).then_inc(GP.h, 1)
        if stop >= 0.02:
            GP.n += 1
            block.gpsimd(p0_gp)

        def p0_touch(sy):
            for ti, (name, t) in enumerate(I.items()):
                ap = t.ap()
                while ap.ndim > 2:
                    ap = ap[0]
                if t.dtype == U32:
                    sy.dma_start(out=murep.ap().bitcast(U32)[0:1, ti:ti + 1],
                                 in_=ap[0:1, 0:1]).then_inc(LD.h, 16)
                else:
                    sy.dma_start(out=murep[0:1, ti:ti + 1], in_=ap[0:1, 0:1]).then_inc(LD.h, 16)
                LD.n += 16
        if (0.03 <= stop < 0.04) or (0.045 <= stop < 7):
            block.sync(p0_touch)

        def p0_sync(sy):
            for i in (1, 2):
                sy.dma_start(out=cst2[:, C_CW[i]:C_CW[i] + 16], in_=I[f'm{i}cw'][:, :]).then_inc(LD.h, 16)
                sy.dma_start(out=cst2[:, C_CB[i]:C_CB[i] + 4], in_=I[f'm{i}cb'][:, :]).then_inc(LD.h, 16)
                sy.dma_start(out=cst2[:, C_A[i]:C_A[i] + 64], in_=I[f'm{i}alog'][:, :]).then_inc(LD.h, 16)
                sy.dma_start(out=cst2[:, C_DV[i]:C_DV[i] + 4], in_=I[f'm{i}dv'][:, :]).then_inc(LD.h, 16)
            for key, col, w in (('ng', C_NG, 8), ('nb', C_NB, 8), ('fg', C_FG, 8),
                                ('fb', C_FB, 8), ('ff1bc', C_F1B, 32), ('ff2bc', C_F2B, 8)):
                sy.dma_start(out=cst2[:, col:col + w], in_=I[key][:, :]).then_inc(LD.h, 16)
            sy.dma_start(out=wrsb[:, :], in_=I['wr'][:, :]).then_inc(LD.h, 16)
            sy.dma_start(out=brsb[:, :], in_=I['br'][:, :]).then_inc(LD.h, 16)
            sy.dma_start(out=qoffs[:, :], in_=I['qoff'][:, :]).then_inc(LD.h, 16)
        if stop >= 0.04:
            LD.n += 17 * 16
            block.sync(p0_sync)

        def p0_scalar(s):
            LD.wait(s)
            s.activation(cst2[:, C_A[1]:C_A[1] + 64], cst2[:, C_A[1]:C_A[1] + 64], AF.Exp)
            s.activation(cst2[:, C_A[2]:C_A[2] + 64], cst2[:, C_A[2]:C_A[2] + 64], AF.Exp).then_inc(SC.h, 1)
        if stop >= 0.05:
            SC.n += 1
            block.scalar(p0_scalar)

        def p0_vec(v):
            SC.wait(v)
            v.tensor_scalar_mul(cst2[:, C_A[1]:C_A[1] + 64], cst2[:, C_A[1]:C_A[1] + 64], -1.0)
            v.tensor_scalar_mul(cst2[:, C_A[2]:C_A[2] + 64], cst2[:, C_A[2]:C_A[2] + 64], -1.0).then_inc(VE.h, 1)
        if stop >= 0.05:
            VE.n += 1
            block.vector(p0_vec)

        if stop >= 0.05:
            gpv, vev = GP.n, VE.n
            def j_t(t):
                t.wait_ge(GP.h, gpv); t.wait_ge(VE.h, vev)
            def j_s(e):
                e.wait_ge(GP.h, gpv); e.wait_ge(VE.h, vev)
            def j_v(v):
                v.wait_ge(GP.h, gpv)
            def j_g(g):
                g.wait_ge(VE.h, vev)
            def j_sy(sy):
                sy.wait_ge(GP.h, gpv); sy.wait_ge(VE.h, vev)
            block.tensor(j_t); block.scalar(j_s); block.vector(j_v)
            block.gpsimd(j_g); block.sync(j_sy)

        def emit_ln(src_dram, tr, rows_dram=None):
            def f_sync(sy):
                VE.wait(sy); SC.wait(sy); MM.wait(sy)
                for kt in range(NKD):
                    sy.dma_start(out=XRAW[:, kt * 512:(kt + 1) * 512],
                                 in_=src_dram[kt * 128:(kt + 1) * 128, tr:tr + 512]).then_inc(LD.h, 16)
            LD.n += NKD * 16
            block.sync(f_sync)

            def f_scal(s):
                LD.wait(s)
                s.activation(scr[:, 0:4096], XRAW[:, 0:4096], AF.Square).then_inc(SC.h, 1)
            SC.n += 1
            block.scalar(f_scal)

            def f_mm(t):
                SC.wait(t)
                for kt in range(NKD):
                    t.matmul(PS1, onesc[:, :], XRAW[:, kt * 512:(kt + 1) * 512],
                             start=(kt == 0), stop=(kt == NKD - 1))
                for kt in range(NKD):
                    ins = t.matmul(PS2, onesc[:, :], scr[:, kt * 512:(kt + 1) * 512],
                                   start=(kt == 0), stop=(kt == NKD - 1))
                ins.then_inc(MM.h, 1)
            MM.n += 1
            block.tensor(f_mm)

            def f_vec(v):
                MM.wait(v)
                v.tensor_scalar_mul(rows[:, 0:512], PS1, 1.0 / D)
                v.tensor_mul(rows[:, 512:1024], rows[:, 0:512], rows[:, 0:512])
                v.scalar_tensor_tensor(rows[:, 1024:1536], PS2, 1.0 / D,
                                       rows[:, 512:1024], ALU.mult, ALU.subtract).then_inc(VE.h, 1)
            VE.n += 1
            block.vector(f_vec)

            def f_scal2(s):
                VE.wait(s)
                s.activation(rows[:, 1024:1536], rows[:, 1024:1536], AF.Sqrt,
                             bias=cst2[0:1, 318:319]).then_inc(SC.h, 1)
            SC.n += 1
            block.scalar(f_scal2)

            def f_vec2(v):
                SC.wait(v)
                v.reciprocal(rows[:, 1536:2048], rows[:, 1024:1536]).then_inc(VE.h, 1)
            VE.n += 1
            block.vector(f_vec2)

            def f_sync2(sy):
                VE.wait(sy)
                sy.dma_start(out=rowb[0:1, :], in_=rows[:, 0:512]).then_inc(LD.h, 16)
                sy.dma_start(out=rowb[1:2, :], in_=rows[:, 1536:2048]).then_inc(LD.h, 16)
                if rows_dram is not None:
                    sy.dma_start(out=rows_dram[0:1, tr:tr + 512], in_=rows[:, 0:512]).then_inc(LD.h, 16)
                    sy.dma_start(out=rows_dram[1:2, tr:tr + 512], in_=rows[:, 1536:2048]).then_inc(LD.h, 16)
            LD.n += 16 * (4 if rows_dram is not None else 2)
            block.sync(f_sync2)

            def f_gp(g):
                LD.wait(g)
                g.dma_start(out=murep[:, :], in_=rowb.ap()[0:1, :].broadcast_to([128, 512])).then_inc(GD.h, 16)
                g.dma_start(out=rrep[:, :], in_=rowb.ap()[1:2, :].broadcast_to([128, 512])).then_inc(GD.h, 16)
            GD.n += 32
            block.gpsimd(f_gp)

            def f_vec3(v):
                GD.wait(v)
                for kt in range(NKD):
                    sl = slice(kt * 512, (kt + 1) * 512)
                    v.tensor_sub(scr[:, sl], XRAW[:, sl], murep[:, :])
                    ins = v.tensor_mul(scr[:, sl], scr[:, sl], rrep[:, :])
                ins.then_inc(VE.h, 1)
            VE.n += 1
            block.vector(f_vec3)

        def emit_ln_routing(src_dram, tail=True, nchunks=4, do_svec=True, do_sgp=True):
            for tr in range(0, 512 * nchunks, 512):
                emit_ln(src_dram, tr)
                if not do_svec:
                    continue
                def s_vec(v, tc=tr // 512):
                    rb = 128 + 8 * tc
                    for kt in range(NKD):
                        sl = slice(kt * 512, (kt + 1) * 512)
                        v.reduce_sum(stt8[:, rb + kt:rb + kt + 1], scr[:, sl], axis=AX.X)
                    ins = v.tensor_copy(stt8[:, 126:127], onesc[:, :])
                    ins.then_inc(VE.h, 1)
                VE.n += 1
                block.vector(s_vec)
                if not do_sgp:
                    continue
                def s_gp(g, tr=tr):
                    VE.wait(g)
                    for kt in range(NKD):
                        g.dma_start(out=lnbuf[kt * 128:(kt + 1) * 128, tr:tr + 512],
                                    in_=scr[:, kt * 512:(kt + 1) * 512]).then_inc(GD.h, 16)
                GD.n += NKD * 16
                block.gpsimd(s_gp)
            if not tail:
                return
            def gp_sum(g):
                VE.wait(g)
                g.tensor_add(stt8[:, 120:128], stt8[:, 128:136], stt8[:, 136:144])
                g.tensor_add(stt8[:, 120:128], stt8[:, 120:128], stt8[:, 144:152])
                ins = g.tensor_add(stt8[:, 120:128], stt8[:, 120:128], stt8[:, 152:160])
                ins.then_inc(GP.h, 1)
            GP.n += 1
            block.gpsimd(gp_sum)
            def r_mm(t):
                VE.wait(t); GP.wait(t)
                for kt in range(NKD):
                    ins = t.matmul(PSB[0:8, 0:1], wrsb[:, kt * 8:(kt + 1) * 8],
                                   stt8[:, 120 + kt:121 + kt], start=(kt == 0), stop=(kt == NKD - 1))
                ins.then_inc(MM.h, 1)
            MM.n += 1
            block.tensor(r_mm)
            def r_vec2(v):
                MM.wait(v)
                v.tensor_scalar_mul(stt8[0:8, 32:33], PSB[0:8, 0:1], 1.0 / TB).then_inc(VE.h, 1)
            VE.n += 1
            block.vector(r_vec2)
            def r_sync(sy):
                VE.wait(sy)
                sy.dma_start(out=scp.ap(), in_=stt8[0:8, 32:33]).then_inc(LD.h, 16)
                if 'dbgA' in dump:
                    sy.dma_start(out=dbgA.ap()[:, 0:16], in_=stt8[:, 0:16]).then_inc(LD.h, 16)
                    sy.dma_start(out=dbgA.ap()[0:8, 16:17], in_=stt8[0:8, 32:33],
                                 allow_slow_non_contiguous=True).then_inc(LD.h, 16)
            LD.n += 16 * (1 + 2 * ('dbgA' in dump))
            block.sync(r_sync)
            ccw = CC.n + 1
            def r_cc(g, cw=ccw):
                LD.wait(g)
                g.collective_compute("AllReduce", ALU.add, replica_groups=GROUPS2,
                                     ins=[scp.ap().opt()], outs=[sco.ap().opt()]).then_inc(CC.h, 1)
                g.wait_ge(CC.h, cw)
                g.dma_start(out=scsb[:, 0:8], in_=sco.ap()).then_inc(GD.h, 16)
            CC.n += 1
            GD.n += 16
            block.gpsimd(r_cc)
            def spacer(v):
                v.tensor_copy(murep[:, 0:512], murep[:, 0:512])
            def a_vec(v):
                GD.wait(v)
                v.tensor_add(scsb[:, 0:8], scsb[:, 0:8], brsb[:, :])
                spacer(v)
                v.reduce_max(scsb[:, 8:9], scsb[:, 0:8], axis=AX.X)
                v.tensor_copy(stt8[0:1, 180:188], scsb[:, 0:8])
                spacer(v)
                v.tensor_copy(scsb[:, 11:12], brsb[:, 0:1]).then_inc(VE.h, 1)
            VE.n += 1
            block.vector(a_vec)
            ld0 = LD.n
            def a_sync0(sy, ld0=ld0):
                VE.wait(sy)
                sy.dma_start(out=rowb[2:3, 0:1], in_=scsb[0:1, 8:9]).then_inc(LD.h, 16)
                sy.wait_ge(LD.h, ld0 + 16)
                sy.dma_start(out=scsb[:, 13:14], in_=rowb[2:3, 0:1]).then_inc(LD.h, 16)
            LD.n += 32
            block.sync(a_sync0)
            def a_vec1b(v):
                LD.wait(v)
                v.tensor_single_scalar(scsb[:, 0:8], scsb[:, 0:8], scsb[:, 13:14], ALU.is_ge)
                spacer(v)
                v.tensor_mul(scsb[:, 0:8], scsb[:, 0:8], iota8[:, :])
                spacer(v)
                v.reduce_sum(scsb[:, 9:10], scsb[:, 0:8], axis=AX.X)
                v.tensor_copy(stt8[0:1, 188:196], scsb[:, 0:8])
                v.tensor_copy(stt8[0:1, 197:198], scsb[:, 13:14])
                spacer(v)
                v.tensor_copy(stt8[0:1, 198:206], iota8[:, :]).then_inc(VE.h, 1)
            VE.n += 1
            block.vector(a_vec1b)
            def a_sync(sy):
                VE.wait(sy)
                sy.dma_start(out=rowb[3:4, 0:1], in_=scsb[0:1, 9:10]).then_inc(LD.h, 16)
                if 'dbgB' in dump:
                    sy.dma_start(out=dbgB.ap(), in_=stt8[:, 96:224]).then_inc(LD.h, 16)
            LD.n += 16 * (2 if 'dbgB' in dump else 1)
            block.sync(a_sync)
            def a_gp(g):
                LD.wait(g)
                g.dma_start(out=stt8[:, 32:33],
                            in_=rowb.ap()[3:4, 0:1].broadcast_to([128, 1])).then_inc(GD.h, 16)
            GD.n += 16
            block.gpsimd(a_gp)
            def a_vec2(v):
                GD.wait(v)
                v.scalar_tensor_tensor(sidxf[:, 0:1], stt8[:, 32:33], float(D), iotap[:, :], ALU.mult, ALU.add)
                v.scalar_tensor_tensor(sidxf[:, 1:2], stt8[:, 32:33], 4096.0, iotap[:, :], ALU.mult, ALU.add)
                v.scalar_tensor_tensor(sidxf[:, 2:3], stt8[:, 32:33], 128.0, iotap[:, :], ALU.mult, ALU.add)
                spacer(v)
                v.tensor_copy(sidx[:, 0:3], sidxf[:, 0:3]).then_inc(VE.h, 1)
            VE.n += 1
            block.vector(a_vec2)

        def emit_moe_gather():
            def g_gp(g):
                VE.wait(g); MM.wait(g)
                w1flat = I['w1'].ap().rearrange("a b k -> (a b) k")
                for kt in range(NKD):
                    for hh in range(2):
                        g.indirect_dma_start(
                            out=wA[:, kt * 4096 + hh * 2048: kt * 4096 + (hh + 1) * 2048],
                            out_offset=None, in_=w1flat,
                            in_offset=bass.IndirectOffsetOnAxis(sidx[:, 0:1], 0),
                            element_offset=(kt * 128) * 4096 + hh * 2048,
                        ).then_inc(GD.h, 16)
                w2flat = I['w2'].ap().rearrange("a b k -> (a b) k")
                for ft in range(32):
                    g.indirect_dma_start(
                        out=wB[:, ft * 1024:(ft + 1) * 1024],
                        out_offset=None, in_=w2flat,
                        in_offset=bass.IndirectOffsetOnAxis(sidx[:, 1:2], 0),
                        element_offset=(ft * 128) * 1024,
                    ).then_inc(GD.h, 16)
                b1flat = I['b1c'].ap().rearrange("a b k -> (a b) k")
                g.indirect_dma_start(out=stt8[:, 64:96], out_offset=None, in_=b1flat,
                                     in_offset=bass.IndirectOffsetOnAxis(sidx[:, 2:3], 0),
                                     ).then_inc(GD.h, 16)
                b2flat = I['b2c'].ap().rearrange("a b k -> (a b) k")
                g.indirect_dma_start(out=stt8[:, 96:104], out_offset=None, in_=b2flat,
                                     in_offset=bass.IndirectOffsetOnAxis(sidx[:, 2:3], 0),
                                     ).then_inc(GD.h, 16)
            GD.n += 50 * 16
            block.gpsimd(g_gp)
            def g_vec(v):
                GD.wait(v)
                v.tensor_copy(cst2[:, C_B1:C_B1 + 32], stt8[:, 64:96])
                v.tensor_scalar_mul(cst2[:, C_B2:C_B2 + 8], stt8[:, 96:104], 0.5).then_inc(VE.h, 1)
            VE.n += 1
            block.vector(g_vec)

        def emit_big_ffn(dst_dram, add_src, hact, b1col, b2col, yscale, add_src2=None):
            for tc in range(4):
                t0 = tc * 512
                mvh = (tc % 2) * 4096
                def mvl_sync(sy, t0=t0, mvh=mvh, tc=tc):
                    if tc >= 2:
                        MM.wait(sy)
                    GD.wait(sy)
                    for kt in range(NKD):
                        sy.dma_start(out=mv[:, mvh + kt * 512: mvh + (kt + 1) * 512],
                                     in_=lnbuf[kt * 128:(kt + 1) * 128, t0:t0 + 512]).then_inc(LD.h, 16)
                LD.n += NKD * 16
                block.sync(mvl_sync)

                m0, s0 = MM.n, SC.n
                def m_mm(t, m0=m0, s0=s0, mvh=mvh):
                    VE.wait(t); SC.wait(t); LD.wait(t)
                    for ft in range(32):
                        if ft >= 2:
                            t.wait_ge(SC.h, s0 + ft - 1)
                        pb = PB[ft % 2]
                        for kt in range(NKD):
                            ins = t.matmul(pb[:, :],
                                           wA[:, kt * 4096 + ft * 128: kt * 4096 + (ft + 1) * 128],
                                           mv[:, mvh + kt * 512: mvh + (kt + 1) * 512],
                                           start=(kt == 0), stop=(kt == NKD - 1))
                        ins.then_inc(MM.h, 1)
                MM.n += 32
                block.tensor(m_mm)
                def m_ev(s, m0=m0, hact=hact, b1col=b1col):
                    for ft in range(32):
                        s.wait_ge(MM.h, m0 + ft + 1)
                        s.activation(scrh[:, ft * 512:(ft + 1) * 512], PB[ft % 2][:, :],
                                     hact, bias=cst2[:, b1col + ft:b1col + ft + 1]).then_inc(SC.h, 1)
                SC.n += 32
                block.scalar(m_ev)

                m1, s1, v1, ld1 = MM.n, SC.n, VE.n, LD.n
                def y_mm(t, s1=s1, v1=v1):
                    t.wait_ge(SC.h, s1)
                    for ot in range(NKD):
                        if ot >= 2:
                            t.wait_ge(VE.h, v1 + ot - 1)
                        pb = PB[2 + ot % 2]
                        for ft in range(32):
                            ins = t.matmul(pb[:, :],
                                           wB[:, ft * 1024 + ot * 128: ft * 1024 + (ot + 1) * 128],
                                           scrh[:, ft * 512:(ft + 1) * 512],
                                           start=(ft == 0), stop=(ft == 31))
                        ins.then_inc(MM.h, 1)
                MM.n += NKD
                block.tensor(y_mm)
                st0 = ST.n
                nload = 2 if add_src2 is not None else 1
                def xio(sy, t0=t0, v1=v1, st0=st0):
                    for ot in range(NKD + 2):
                        if 2 <= ot:
                            sy.wait_ge(VE.h, v1 + (ot - 2) + 1)
                            sy.dma_start(out=dst_dram[(ot - 2) * 128:(ot - 1) * 128, t0:t0 + 512],
                                         in_=stt8[:, 512 + (ot % 2) * 512: 1024 + (ot % 2) * 512]).then_inc(ST.h, 16)
                        if ot < NKD:
                            if ot >= 2:
                                sy.wait_ge(ST.h, st0 + (ot - 1) * 16)
                            sy.dma_start(out=stt8[:, 512 + (ot % 2) * 512: 1024 + (ot % 2) * 512],
                                         in_=add_src[ot * 128:(ot + 1) * 128, t0:t0 + 512]).then_inc(LD.h, 16)
                            if add_src2 is not None:
                                sy.dma_start(out=cvb[:, (ot % 2) * 512:(ot % 2) * 512 + 512],
                                             in_=add_src2[ot * 128:(ot + 1) * 128, t0:t0 + 512]).then_inc(LD.h, 16)
                LD.n += NKD * 16 * nload
                ST.n += NKD * 16
                block.sync(xio)
                def y_add(v, m1=m1, ld1=ld1, b2col=b2col, yscale=yscale):
                    for ot in range(NKD):
                        v.wait_ge(MM.h, m1 + ot + 1)
                        v.wait_ge(LD.h, ld1 + (ot + 1) * 16 * nload)
                        v.tensor_scalar(stt8[:, 0:512], PB[2 + ot % 2][:, :],
                                        yscale, cst2[:, b2col + ot:b2col + ot + 1],
                                        ALU.mult, ALU.add)
                        sl = slice(512 + (ot % 2) * 512, 1024 + (ot % 2) * 512)
                        if add_src2 is not None:
                            v.tensor_add(stt8[:, sl], stt8[:, sl], stt8[:, 0:512])
                            v.tensor_add(stt8[:, sl], stt8[:, sl],
                                         cvb[:, (ot % 2) * 512:(ot % 2) * 512 + 512]).then_inc(VE.h, 1)
                        else:
                            v.tensor_add(stt8[:, sl], stt8[:, sl], stt8[:, 0:512]).then_inc(VE.h, 1)
                VE.n += NKD
                block.vector(y_add)

        def emit_mamba(mi):
            mz = mi - 1
            rev = (mi == 2)
            def w_gp(g):
                ST.wait(g); VE.wait(g); MM.wait(g); SC.wait(g)
                for part, key in ((0, f'm{mi}wxsT'), (1, f'm{mi}wzT')):
                    for kt in range(NKD):
                        g.dma_start(out=wB[:, part * 4096 + kt * 512: part * 4096 + (kt + 1) * 512],
                                    in_=I[key][kt * 128:(kt + 1) * 128, :]).then_inc(GD.h, 16)
            GD.n += 16 * 16
            block.gpsimd(w_gp)
            def w_sync(sy):
                MM.wait(sy); VE.wait(sy)
                for j in range(NDT):
                    sy.dma_start(out=xpasb[:, j * 104:(j + 1) * 104],
                                 in_=I[f'm{mi}xpa'][j * 128:(j + 1) * 128, :]).then_inc(LD.h, 16)
                sy.dma_start(out=dtwasb[:, :], in_=I[f'm{mi}dtwa'][:, :]).then_inc(LD.h, 16)
            LD.n += 5 * 16
            block.sync(w_sync)

            for ch in range(TB // CH):
                tok0 = ch * CH
                gtok = (TB - (ch + 1) * CH) if rev else tok0
                rk, off = gtok // TQ, gtok % TQ
                mr = slice(tok0, tok0 + CH)

                def c_sync(sy, rk=rk, off=off):
                    VE.wait(sy); MM.wait(sy); CC.wait(sy)
                    for kt in range(NKD):
                        sy.dma_start(out=HM(kt),
                                     in_=hall[rk, kt * 128:(kt + 1) * 128, off:off + CH]).then_inc(LD.h, 16)
                LD.n += NKD * 16
                block.sync(c_sync)

                m0, s0 = MM.n, SC.n
                def c_mm(t, m0=m0, s0=s0, rev=rev):
                    LD.wait(t); GD.wait(t)
                    gi = 0
                    for pt in range(8):
                        part, psub = (0, pt) if pt < 4 else (1, pt - 4)
                        for ff in range(2):
                            if gi >= 2:
                                t.wait_ge(SC.h, s0 + gi - 1)
                            pb = PB[gi % 2]
                            for kt in range(NKD):
                                if not rev:
                                    mvap = HM(kt)[:, ff * 512:(ff + 1) * 512]
                                else:
                                    mvap = HM(kt)[:, 512 * (1 - ff):512 * (2 - ff)][:, ::-1]
                                ins = t.matmul(pb[:, :],
                                               wB[:, part * 4096 + kt * 512 + psub * 128:
                                                  part * 4096 + kt * 512 + (psub + 1) * 128],
                                               mvap, start=(kt == 0), stop=(kt == NKD - 1))
                            ins.then_inc(MM.h, 1)
                            gi += 1
                MM.n += 16
                block.tensor(c_mm)
                def c_ev(s, m0=m0):
                    gi = 0
                    for pt in range(8):
                        part, psub = (0, pt) if pt < 4 else (1, pt - 4)
                        for ff in range(2):
                            s.wait_ge(MM.h, m0 + gi + 1)
                            dst = (XS(psub) if part == 0 else ZZ(psub))[:, ff * 512:(ff + 1) * 512]
                            s.activation(dst, PB[gi % 2][:, :], AF.Copy).then_inc(SC.h, 1)
                            gi += 1
                SC.n += 16
                block.scalar(c_ev)

                for j in range(NDT):
                    hcol = 128 + mz * 12 + j * 3
                    def cv_vec(v, j=j, hcol=hcol, mi=mi):
                        SC.wait(v); LD.wait(v)
                        v.tensor_copy(cvb[:, 0:3], state[:, hcol:hcol + 3])
                        v.tensor_copy(cvb[:, 3:1027], XS(j))
                        v.tensor_copy(state[:, hcol:hcol + 3], cvb[:, 1024:1027])
                        v.tensor_scalar(cvb[:, 1028:2052], cvb[:, 0:1024],
                                        cst2[:, C_CW[mi] + j * 4:C_CW[mi] + j * 4 + 1], None, ALU.mult)
                        for jj in range(1, 4):
                            ins = v.scalar_tensor_tensor(cvb[:, 1028:2052], cvb[:, jj:jj + 1024],
                                                   cst2[:, C_CW[mi] + j * 4 + jj:C_CW[mi] + j * 4 + jj + 1],
                                                   cvb[:, 1028:2052], ALU.mult, ALU.add)
                        ins.then_inc(VE.h, 1)
                    VE.n += 1
                    block.vector(cv_vec)
                    def cv_scal(s, j=j, mi=mi):
                        VE.wait(s)
                        s.activation(XS(j), cvb[:, 1028:2052], AF.Silu,
                                     bias=cst2[:, C_CB[mi] + j:C_CB[mi] + j + 1]).then_inc(SC.h, 1)
                    SC.n += 1
                    block.scalar(cv_scal)

                def xp_mm(t):
                    SC.wait(t); VE.wait(t)
                    for ff in range(2):
                        for j in range(NDT):
                            ins = t.matmul(PB[4 + ff][0:104, :], xpasb[:, j * 104:(j + 1) * 104],
                                           XS(j)[:, ff * 512:(ff + 1) * 512],
                                           start=(j == 0), stop=(j == NDT - 1))
                        ins.then_inc(MM.h, 1)
                MM.n += 2
                mmx = MM.n
                block.tensor(xp_mm)
                def xp_vec(v, mmx=mmx):
                    v.wait_ge(MM.h, mmx)
                    v.tensor_copy(xpst[:, 0:512], PB[4][0:104, :])
                    v.tensor_copy(xpst[:, 512:1024], PB[5][0:104, :]).then_inc(VE.h, 1)
                VE.n += 1
                block.vector(xp_vec)
                def xp_st(sy, mz=mz, ch=ch):
                    VE.wait(sy)
                    sy.dma_start(out=dblp[mz][ch].ap(), in_=xpst[:, :]).then_inc(LD.h, 16)
                LD.n += 16
                block.sync(xp_st)

                ccw = CC.n + 1
                def xp_cc(g, mz=mz, ch=ch, cw=ccw):
                    LD.wait(g)
                    g.collective_compute("AllReduce", ALU.add, replica_groups=GROUPS2,
                                         ins=[dblp[mz][ch].ap().opt()],
                                         outs=[dblo[mz][ch].ap().opt()]).then_inc(CC.h, 1)
                    g.wait_ge(CC.h, cw)
                CC.n += 1
                block.gpsimd(xp_cc)

                def dl_sync(sy, mz=mz, ch=ch):
                    CC.wait(sy); MM.wait(sy)
                    sy.dma_start(out=dblsb[0:64, :], in_=dblo[mz][ch][0:64, :]).then_inc(LD.h, 16)
                LD.n += 16
                block.sync(dl_sync)

                for pt in range(NDT):
                    for ff in range(2):
                        mmd = MM.n + 1
                        def dt_mm(t, pt=pt, ff=ff):
                            LD.wait(t); VE.wait(t)
                            t.matmul(PB[5][:, :], dtwasb[:, pt * 128:(pt + 1) * 128],
                                     dblsb[0:65, ff * 512:(ff + 1) * 512],
                                     start=True, stop=True).then_inc(MM.h, 1)
                        MM.n += 1
                        block.tensor(dt_mm)
                        def dt_sc(s, mmd=mmd):
                            s.wait_ge(MM.h, mmd)
                            s.activation(stt8[:, 0:512], PB[5][:, :], AF.Abs)
                            s.activation(stt8[:, 0:512], stt8[:, 0:512], AF.Exp, scale=-1.0)
                            s.activation(stt8[:, 0:512], stt8[:, 0:512], AF.Ln, bias=1.0).then_inc(SC.h, 1)
                        SC.n += 1
                        block.scalar(dt_sc)
                        def dt_ve(v, pt=pt, ff=ff):
                            SC.wait(v)
                            v.tensor_scalar_max(stt8[:, 512:1024], PB[5][:, :], 0.0)
                            v.tensor_add(DT(pt)[:, ff * 512:(ff + 1) * 512],
                                         stt8[:, 0:512], stt8[:, 512:1024]).then_inc(VE.h, 1)
                        VE.n += 1
                        block.vector(dt_ve)

                def dx_vec(v):
                    for j in range(NDT):
                        v.tensor_mul(DTX(j), DT(j), XS(j))
                        ins = v.memset(YA(j), 0.0)
                    ins.then_inc(VE.h, 1)
                VE.n += 1
                block.vector(dx_vec)

                rp0 = REP.n
                vscan0 = VE.n + 1
                def rep_gp(g, mz=mz, ch=ch, vscan0=vscan0):
                    VE.wait(g)
                    for si in range(DS):
                        buf = REPB[si % 2]
                        if si >= 2:
                            g.wait_ge(VE.h, vscan0 + (si - 1) * NDT)
                        g.dma_start(out=buf[:, 0:CH],
                                    in_=dblo[mz][ch].ap()[65 + si:66 + si, :].broadcast_to([128, CH])).then_inc(REP.h, 16)
                        g.dma_start(out=buf[:, CH:2 * CH],
                                    in_=dblo[mz][ch].ap()[81 + si:82 + si, :].broadcast_to([128, CH])).then_inc(REP.h, 16)
                REP.n += DS * 32
                block.gpsimd(rep_gp)

                sc0, v0 = SC.n, VE.n
                def sc_scal(s, mi=mi, sc0=sc0, v0=v0):
                    VE.wait(s, v0)
                    k = 0
                    for si in range(DS):
                        for j in range(NDT):
                            if k >= 2:
                                s.wait_ge(VE.h, v0 + k - 1)
                            acol = C_A[mi] + j * DS + si
                            s.activation(DAB[k % 2], DT(j), AF.Exp,
                                         scale=cst2[:, acol:acol + 1]).then_inc(SC.h, 1)
                            k += 1
                SC.n += DS * NDT
                block.scalar(sc_scal)

                def sc_vec(v, mz=mz, ch=ch, sc0=sc0, v0=v0, rp0=rp0):
                    k = 0
                    for si in range(DS):
                        buf = REPB[si % 2]
                        v.wait_ge(REP.h, rp0 + 32 * (si + 1))
                        for j in range(NDT):
                            v.wait_ge(SC.h, sc0 + k + 1)
                            scol = mz * 64 + j * DS + si
                            v.tensor_mul(DBX, DTX(j), buf[:, 0:CH])
                            if ch == 0:
                                v.tensor_tensor_scan(HH, DAB[k % 2], DBX, 0.0, ALU.mult, ALU.add)
                            else:
                                v.tensor_tensor_scan(HH, DAB[k % 2], DBX, state[:, scol:scol + 1],
                                                     ALU.mult, ALU.add)
                            v.tensor_mul(DBX, HH, buf[:, CH:2 * CH])
                            v.tensor_add(YA(j), YA(j), DBX)
                            v.tensor_copy(state[:, scol:scol + 1], HH[:, CH - 1:CH]).then_inc(VE.h, 1)
                            k += 1
                VE.n += DS * NDT
                block.vector(sc_vec)

                def g_scal(s):
                    VE.wait(s)
                    for j in range(NDT):
                        ins = s.activation(DT(j), ZZ(j), AF.Silu)
                    ins.then_inc(SC.h, 1)
                SC.n += 1
                block.scalar(g_scal)
                def g_vec(v, mi=mi):
                    SC.wait(v)
                    for j in range(NDT):
                        v.scalar_tensor_tensor(YA(j), XS(j), cst2[:, C_DV[mi] + j:C_DV[mi] + j + 1],
                                               YA(j), ALU.mult, ALU.add)
                        ins = v.tensor_mul(YA(j), YA(j), DT(j))
                    ins.then_inc(VE.h, 1)
                VE.n += 1
                block.vector(g_vec)
                def g_gp2(g, mz=mz, mr=mr):
                    VE.wait(g)
                    for j in range(NDT):
                        g.dma_start(out=yin[mz, j * 128:(j + 1) * 128, mr], in_=YA(j)).then_inc(GD.h, 16)
                GD.n += NDT * 16
                block.gpsimd(g_gp2)

            ccw = CC.n + 1
            def yag(g, mz=mz, cw=ccw):
                GD.wait(g)
                g.collective_compute("AllGather", ALU.bypass, replica_groups=GROUPS2,
                                     ins=[yin.ap()[mz].opt()],
                                     outs=[yout.ap()[mz * 4 * DQ:(mz + 1) * 4 * DQ, :].opt()]).then_inc(CC.h, 1)
                g.wait_ge(CC.h, cw)
            CC.n += 1
            block.gpsimd(yag)

        def emit_outproj():
            def ow_gp(g):
                VE.wait(g); MM.wait(g); SC.wait(g); ST.wait(g)
                for mz in range(2):
                    for kt in range(16):
                        g.dma_start(out=wA[:, mz * 16384 + kt * 1024: mz * 16384 + (kt + 1) * 1024],
                                    in_=I[f'm{mz + 1}owT'][kt * 128:(kt + 1) * 128, :]).then_inc(GD.h, 16)
            GD.n += 32 * 16
            block.gpsimd(ow_gp)
            for tc in range(4):
                t0 = tc * 512
                m0, v0, s0 = MM.n, VE.n, SC.n
                for mz in range(2):
                    def ym_gp(g, t0=t0, mz=mz, m0=m0):
                        if mz == 0:
                            VE.wait(g); CC.wait(g)
                            MM.wait(g)
                        else:
                            g.wait_ge(MM.h, m0 + NKD)
                        with g.register(f'rq{t0}_{mz}') as rq, g.register(f'rr{t0}_{mz}') as rr:
                            g.reg_load(rq, qoffs[0:1, 0:1])
                            g.reg_add(rr, rq, mz * 4 * DQ * TB + t0)
                            src = bass.AP(yout, rr, [[TB, 128], [128 * TB, 16], [1, 512]])
                            g.dma_start(out=mv.ap()[:, 0:8192],
                                        in_=src).then_inc(GD.h, 16)
                    GD.n += 16
                    block.gpsimd(ym_gp)
                    gd0 = GD.n
                    def w_mm(t, mz=mz, gd0=gd0):
                        t.wait_ge(GD.h, gd0)
                        for ot in range(NKD):
                            for kt in range(16):
                                ins = t.matmul(PB[ot][:, :],
                                               wA[:, mz * 16384 + kt * 1024 + ot * 128:
                                                  mz * 16384 + kt * 1024 + (ot + 1) * 128],
                                               mv[:, kt * 512:(kt + 1) * 512],
                                               start=(mz == 0 and kt == 0),
                                               stop=(mz == 1 and kt == 15))
                            ins.then_inc(MM.h, 1)
                    MM.n += NKD
                    block.tensor(w_mm)
                def rw_sync(sy, t0=t0):
                    sy.dma_start(out=rowb[0:1, :], in_=lnrows1[0:1, t0:t0 + 512]).then_inc(LD.h, 16)
                    sy.dma_start(out=rowb[1:2, :], in_=lnrows1[1:2, t0:t0 + 512]).then_inc(LD.h, 16)
                LD.n += 32
                block.sync(rw_sync)
                def rw_gp(g):
                    LD.wait(g); VE.wait(g)
                    g.dma_start(out=murep[:, :], in_=rowb.ap()[0:1, :].broadcast_to([128, 512])).then_inc(GD.h, 16)
                    g.dma_start(out=rrep[:, :], in_=rowb.ap()[1:2, :].broadcast_to([128, 512])).then_inc(GD.h, 16)
                GD.n += 32
                block.gpsimd(rw_gp)
                ld0 = LD.n
                gd1 = GD.n
                XSL = [stt8[:, 1024:1536], cvb[:, 0:512]]
                USL = [cvb[:, 512:1024], cvb[:, 1024:1536]]
                st1 = ST.n
                def xio(sy, t0=t0, v0=v0, gd1=gd1, st1=st1):
                    sy.wait_ge(GD.h, gd1)
                    for ot in range(NKD + 2):
                        if ot >= 2:
                            sy.wait_ge(VE.h, v0 + 2 * (ot - 2) + 2)
                            sy.dma_start(out=ubuf[(ot - 2) * 128:(ot - 1) * 128, t0:t0 + 512],
                                         in_=USL[ot % 2]).then_inc(ST.h, 16)
                        if ot < NKD:
                            if ot >= 2:
                                sy.wait_ge(ST.h, st1 + (ot - 1) * 16)
                            sy.dma_start(out=XSL[ot % 2],
                                         in_=xres[ot * 128:(ot + 1) * 128, t0:t0 + 512]).then_inc(LD.h, 16)
                LD.n += NKD * 16
                ST.n += NKD * 16
                block.sync(xio)
                m2n = MM.n
                def op_vec(v, m2n=m2n, s0=s0, v0=v0, ld0=ld0, st1=st1):
                    for ot in range(NKD):
                        v.wait_ge(MM.h, m2n - NKD + ot + 1)
                        v.wait_ge(LD.h, ld0 + (ot + 1) * 16)
                        if ot >= 2:
                            v.wait_ge(ST.h, st1 + (ot - 1) * 16)
                        v.tensor_sub(stt8[:, 0:512], XSL[ot % 2], murep[:, :])
                        v.tensor_mul(stt8[:, 0:512], stt8[:, 0:512], rrep[:, :])
                        v.tensor_add(USL[ot % 2], stt8[:, 0:512], PB[ot][:, :]).then_inc(VE.h, 1)
                        v.wait_ge(SC.h, s0 + ot + 1)
                        v.tensor_add(USL[ot % 2], stt8[:, 0:512],
                                     stt8[:, 512:1024]).then_inc(VE.h, 1)
                VE.n += 2 * NKD
                block.vector(op_vec)
                def op_scal(s, v0=v0):
                    for ot in range(NKD):
                        s.wait_ge(VE.h, v0 + 2 * ot + 1)
                        s.activation(stt8[:, 512:1024], USL[ot % 2], AF.Gelu).then_inc(SC.h, 1)
                SC.n += NKD
                block.scalar(op_scal)

        def emit_ffn():
            def fw_gp(g):
                ST.wait(g); MM.wait(g); SC.wait(g); VE.wait(g)
                for kt in range(NKD):
                    g.dma_start(out=wA[:, kt * 4096:(kt + 1) * 4096],
                                in_=I['ff1T'][kt * 128:(kt + 1) * 128, :]).then_inc(GD.h, 16)
                for ft in range(32):
                    g.dma_start(out=wB[:, ft * 1024:(ft + 1) * 1024],
                                in_=I['ff2T'][ft * 128:(ft + 1) * 128, :]).then_inc(GD.h, 16)
            GD.n += 40 * 16
            block.gpsimd(fw_gp)
            for tr in range(0, TQ, 512):
                emit_ln(ubuf, tr)
                def l2_vec(v):
                    for kt in range(NKD):
                        sl = slice(kt * 512, (kt + 1) * 512)
                        ins = v.tensor_scalar(scr[:, sl], scr[:, sl],
                                              cst2[:, C_FG + kt:C_FG + kt + 1],
                                              cst2[:, C_FB + kt:C_FB + kt + 1],
                                              ALU.mult, ALU.add)
                    ins.then_inc(VE.h, 1)
                VE.n += 1
                block.vector(l2_vec)
                def l2_gp(g, tr=tr):
                    VE.wait(g)
                    for kt in range(NKD):
                        g.dma_start(out=lnbuf[kt * 128:(kt + 1) * 128, tr:tr + 512],
                                    in_=scr[:, kt * 512:(kt + 1) * 512]).then_inc(GD.h, 16)
                GD.n += NKD * 16
                block.gpsimd(l2_gp)
            emit_big_ffn(xres, xres, AF.Gelu, C_F1B, C_F2B, 1.0, add_src2=ubuf)

        if 0.25 <= stop < 0.3:
            emit_ln_routing(I['xT'], tail=False)
        if stop >= 0.3:
            emit_ln_routing(I['xT'])
        if stop >= 0.6:
            emit_moe_gather()
        if stop >= 1:
            emit_big_ffn(xres, I['xT'], AF.Relu, C_B1, C_B2, 0.5)

        if stop >= 2:
            for tr in range(0, TQ, 512):
                emit_ln(xres, tr, rows_dram=lnrows1)
                def h_vec(v):
                    for kt in range(NKD):
                        sl = slice(kt * 512, (kt + 1) * 512)
                        ins = v.tensor_scalar(scr[:, sl], scr[:, sl],
                                              cst2[:, C_NG + kt:C_NG + kt + 1],
                                              cst2[:, C_NB + kt:C_NB + kt + 1],
                                              ALU.mult, ALU.add)
                    ins.then_inc(VE.h, 1)
                VE.n += 1
                block.vector(h_vec)
                def h_gp(g, tr=tr):
                    VE.wait(g)
                    for kt in range(NKD):
                        g.dma_start(out=h1buf[kt * 128:(kt + 1) * 128, tr:tr + 512],
                                    in_=scr[:, kt * 512:(kt + 1) * 512]).then_inc(GD.h, 16)
                GD.n += NKD * 16
                block.gpsimd(h_gp)
            ccw = CC.n + 1
            def hag(g, cw=ccw):
                GD.wait(g)
                g.collective_compute("AllGather", ALU.bypass, replica_groups=GROUPS2,
                                     ins=[h1buf.ap().opt()],
                                     outs=[hall.ap().opt()]).then_inc(CC.h, 1)
                g.wait_ge(CC.h, cw)
            CC.n += 1
            block.gpsimd(hag)

        if stop >= 3:
            emit_mamba(1)
        if stop >= 4:
            emit_mamba(2)
        if stop >= 5:
            emit_outproj()
        if stop >= 6:
            emit_ffn()
        if stop >= 7:
            emit_ln_routing(xres)
            emit_moe_gather()
            emit_big_ffn(out, xres, AF.Relu, C_B1, C_B2, 0.5)
        else:
            def dv_sync(sy):
                VE.wait(sy); SC.wait(sy); MM.wait(sy); GD.wait(sy)
                src = xres if stop >= 1 else I['xT']
                sy.dma_start(out=out.ap(), in_=src.ap()).then_inc(ST.h, 16)
            ST.n += 16
            block.sync(dv_sync)

        def fin_gp(g):
            for s in (MM, VE, SC, CC, LD):
                s.wait(g)
            if ST.n:
                g.wait_ge(ST.h, ST.n)
            if GD.n:
                g.wait_ge(GD.h, GD.n)
            for t, o in dumps:
                g.dma_start(out=o.ap(), in_=t.ap()).then_inc(GD.h, 16)
                GD.n += 16
            if GD.n:
                g.wait_ge(GD.h, GD.n)
            g.nop().then_inc(GP.h, 1)
        GP.n += 1
        block.gpsimd(fin_gp)

        def fin_sync(sy):
            GP.wait(sy)
            sy.wait_ge(ST.h, ST.n)
        block.sync(fin_sync)

    return nc, dumps


def kernel(**inputs):
    nc, _ = build()
    in_maps = [_prep_core_inputs(inputs, c) for c in range(NC)]
    res = run_bass_kernel_spmd(nc, in_maps, core_ids=list(range(NC)))
    outs = [np.asarray(res.results[c]['out']).T for c in range(NC)]
    return np.concatenate(outs, axis=0).reshape(2, TB, D)
